# version 01
# baseline (speedup 1.0000x reference)
import numpy as np
import ml_dtypes
import concourse.bass as bass
import concourse.mybir as mybir
from concourse.bass_utils import run_bass_kernel_spmd

F32 = mybir.dt.float32
F16 = mybir.dt.float16
BF16 = mybir.dt.bfloat16
AF = mybir.ActivationFunctionType
ALU = mybir.AluOpType
AX = mybir.AxisListType

NCORE = 8
TOK = 4096
NGRP = 2
GTOK = TOK // NGRP
NBLK = GTOK // 128
NTIL = GTOK // 512
D = 1024
DC = D // 128
HID = 2048
HC = HID // 128
NE = 8
B2_DEPTH = 99

XN_O = 0
WI_O = 16384
WO_O = 32768
HT_O = 49152
XNT_O = 57344
KSB_O = 59392
K_SB_O = KSB_O + 2080
V_SB_O = KSB_O + 3104
GFM_O = KSB_O + 2080
GTM_O = KSB_O + 4160
BROW_O = 65664
KVBF_O = 67712
BCB_O = 68752
ONE_O = 70800
IDT_O = 70928
ABF_COLS = 71056

XB_O = 0
LNS_O = 3072
LNS2_O = 4096
KVS_O = 5120
LNR_O = 6160
ONF_O = 7184
MV_O = 7312
RSD_O = 7344
ST6_O = 7360
SQT_O = 7376
GATES_O = 7392
LOGI_O = 7520
M1_O = 7648
M2_O = 7664
MB_O = 7680
EQ_O = 7808
EX_O = 7936
SS_O = 8064
RH_O = 8080
DEN_O = 8096
RDN_O = 8112
BCL_O = 8128
AF32_COLS = 8160

AH_COLS = 16384


class Ctr:
    def __init__(self, h):
        self.h = h
        self.n = 0

    def inc(self, k=1):
        self.n += k
        return self.n

    def mark(self):
        return (self, self.n)


def _w(eng, mark):
    if mark is None:
        return
    if isinstance(mark, list):
        for m in mark:
            _w(eng, m)
        return
    ctr, val = mark
    assert val <= ctr.n, "wait on not-yet-emitted semaphore value"
    if val > 0:
        eng.wait_ge(ctr.h, val)


class Builder:
    pass


def build_program(phases=('A', 'B', 'C')):
    nc = bass.Bass("TRN2", target_bir_lowering=False, debug=False)

    dr = {}
    dr['xin'] = nc.dram_tensor("xcore", [TOK, D], F32, kind="ExternalInput")
    dr['xout'] = nc.dram_tensor("xout", [TOK, D], F32, kind="ExternalOutput")
    dr['w_in'] = nc.dram_tensor("w_in", [NE, D, HID], F32, kind="ExternalInput")
    dr['w_out'] = nc.dram_tensor("w_out", [NE, HID, D], F32, kind="ExternalInput")
    dr['wg1'] = nc.dram_tensor("wg1", [D, 256], F32, kind="ExternalInput")
    dr['wg2'] = nc.dram_tensor("wg2", [256, NE], F32, kind="ExternalInput")
    for w in ['wq', 'wk', 'wv', 'wh', 'wo_a']:
        dr[w] = nc.dram_tensor(w, [D, D], F32, kind="ExternalInput")
    dr['brows'] = nc.dram_tensor("brows", [128, 2048], F32, kind="ExternalInput")
    dr['lnrows'] = nc.dram_tensor("lnrows", [128, 1024], F32, kind="ExternalInput")
    dr['bcols'] = nc.dram_tensor("bcols", [128, 16], F32, kind="ExternalInput")
    dr['ident'] = nc.dram_tensor("ident", [128, 128], BF16, kind="ExternalInput")
    dr['x1'] = nc.dram_tensor("x1", [TOK, D], F32)
    dr['x2'] = nc.dram_tensor("x2", [TOK, D], F32)
    dr['xnscr'] = nc.dram_tensor("xnscr", [NGRP, 128, 16384], BF16)
    dr['cc_in'] = nc.dram_tensor("cc_in", [128, 1040], F32)
    dr['cc_out'] = nc.dram_tensor("cc_out", [128, 1040], F32)

    from contextlib import ExitStack
    with ExitStack() as st:
        B = Builder()
        B.nc = nc
        B.abf = st.enter_context(nc.sbuf_tensor("abf", [128, ABF_COLS], BF16))
        B.af = st.enter_context(nc.sbuf_tensor("af32", [128, AF32_COLS], F32))
        B.ah = st.enter_context(nc.sbuf_tensor("ah", [128, AH_COLS], F16))
        B.pb = [st.enter_context(nc.psum_tensor(f"pb{i}", [128, 512], F32))
                for i in range(7)]
        B.pbt = st.enter_context(nc.psum_tensor("pbt", [128, 1024], BF16))
        B.cT = Ctr(st.enter_context(nc.semaphore("sT")))
        B.cV = Ctr(st.enter_context(nc.semaphore("sV")))
        B.cS = Ctr(st.enter_context(nc.semaphore("sS")))
        B.cG = Ctr(st.enter_context(nc.semaphore("sG")))
        B.cCC = Ctr(st.enter_context(nc.semaphore("sCC")))
        B.dC = Ctr(st.enter_context(nc.semaphore("dC")))
        B.dWI = Ctr(st.enter_context(nc.semaphore("dWI")))
        B.dWO = Ctr(st.enter_context(nc.semaphore("dWO")))
        B.dXB = [Ctr(st.enter_context(nc.semaphore(f"dXB{i}"))) for i in range(3)]
        B.dST = [Ctr(st.enter_context(nc.semaphore(f"dST{i}"))) for i in range(2)]
        B.dXN = Ctr(st.enter_context(nc.semaphore("dXN")))
        B.dCC = Ctr(st.enter_context(nc.semaphore("dCC")))
        B.block = st.enter_context(nc.Block())

        B.bank_free = {i: None for i in range(7)}
        B.b7_free = None
        B.xblk_free = [None, None, None]
        B.xnt_free = [None, None]
        B.lns_free = [None, None]
        B.gfm_free = [None, None]
        B.gtm_free = [None, None]
        B.wi_free = None
        B.wo_free = None
        B.wi_ready = None
        B.wo_ready = None
        B.xn_free = []
        B.aggr_prev = None
        B.sqt_free = None

        _setup(B, dr)
        if 'A' in phases:
            dst = dr['x1'] if ('B' in phases or 'C' in phases) else dr['xout']
            ffn_phase(B, dr, src=dr['xin'], dst=dst)
        if 'B' in phases or 'b' in phases:
            src = dr['x1'] if 'A' in phases else dr['xin']
            dst = dr['x2'] if 'C' in phases else dr['xout']
            attn_phase(B, dr, src=src, dst=dst, b1_only=('b' in phases))
        if 'C' in phases:
            src = dr['x2'] if 'B' in phases else (dr['x1'] if 'A' in phases else dr['xin'])
            ffn_phase(B, dr, src=src, dst=dr['xout'])

        @B.block.sync
        def _(sy):
            for c in [B.dC, B.dWI, B.dWO, B.dXN, B.dCC] + B.dXB + B.dST:
                if c.n:
                    sy.wait_ge(c.h, c.n)
            if B.cCC.n:
                sy.wait_ge(B.cCC.h, B.cCC.n)
    return nc


def _setup(B, dr):
    abf, af, pb = B.abf, B.af, B.pb
    block = B.block
    cT, cV, dC = B.cT, B.cV, B.dC

    @block.gpsimd
    def _(g):
        g.dma_start(out=abf[:, BROW_O:BROW_O + 2048], in_=dr['brows'].ap()).then_inc(dC.h, 16)
        g.dma_start(out=af[:, LNR_O:LNR_O + 1024], in_=dr['lnrows'].ap()).then_inc(dC.h, 16)
        g.dma_start(out=af[:, BCL_O:BCL_O + 16], in_=dr['bcols'].ap()).then_inc(dC.h, 16)
        g.dma_start(out=abf[:, IDT_O:IDT_O + 128], in_=dr['ident'].ap()).then_inc(dC.h, 16)
        g.dma_start(out=abf[:, KSB_O:KSB_O + 2048].rearrange("p (c g) -> p c g", c=DC),
                    in_=dr['wg1'].ap().rearrange("(c p) g -> p c g", p=128)).then_inc(dC.h, 16)
        g.dma_start(out=abf[:, KSB_O + 2048:KSB_O + 2064].rearrange("p (c g) -> p c g", c=2),
                    in_=dr['wg2'].ap().rearrange("(c p) g -> p c g", p=128)).then_inc(dC.h, 16)
    dC.inc(6 * 16)
    B.consts = dC.mark()

    @block.vector
    def _(v):
        v.memset(abf[:, ONE_O:ONE_O + 128], 1.0)
        v.memset(af[:, ONF_O:ONF_O + 128], 1.0)
        v.memset(abf[:, V_SB_O:V_SB_O + 1040]
                 .rearrange("p (h c) -> p h c", c=65)[:, :, 64:65], 1.0).then_inc(cV.h, 1)
    ones_done = (cV, cV.inc())

    @block.tensor
    def _(t):
        _w(t, B.consts)
        _w(t, ones_done)
        for j, pbase in enumerate([0, 32]):
            for nt in range(2):
                t.matmul(pb[j * 2 + nt][:, 0:512],
                         af[pbase:pbase + 1, ONF_O:ONF_O + 128],
                         af[pbase:pbase + 1, LNR_O + nt * 512:LNR_O + (nt + 1) * 512],
                         start=True, stop=True).then_inc(cT.h, 1)
                cT.inc()
    bc_mm = cT.mark()

    @block.vector
    def _(v):
        _w(v, bc_mm)
        for j in range(2):
            for nt in range(2):
                ins = v.tensor_copy(abf[:, BCB_O + j * 1024 + nt * 512:BCB_O + j * 1024 + (nt + 1) * 512],
                                    pb[j * 2 + nt][:, 0:512])
        ins.then_inc(cV.h, 1)
    bc_done = (cV, cV.inc())
    for i in range(4):
        B.bank_free[i] = bc_done


def ln_pass(B, dr, src, g):
    abf, af, pbt = B.abf, B.af, B.pbt
    block = B.block
    cT, cV, cS = B.cT, B.cV, B.cS

    for blk in range(NBLK):
        slot = blk % 3
        dxb = B.dXB[slot]
        tok0 = g * GTOK + blk * 128

        @block.sync
        def _(sy, slot=slot, tok0=tok0):
            _w(sy, B.xblk_free[slot])
            sy.dma_start(out=af[:, XB_O + slot * 1024:XB_O + (slot + 1) * 1024],
                         in_=src.ap()[tok0:tok0 + 128, :]).then_inc(dxb.h, 16)
        ld = (dxb, dxb.inc(16))

        @block.vector
        def _(v, blk=blk, slot=slot):
            _w(v, ld)
            if blk == 0:
                _w(v, B.consts)
            _w(v, B.aggr_prev)
            xb = af[:, XB_O + slot * 1024:XB_O + (slot + 1) * 1024]
            v.bn_stats(af[:, ST6_O:ST6_O + 6], xb[:, 0:512]).then_inc(cV.h, 1)
            v.wait_ge(cV.h, cV.inc())
            v.bn_stats(af[:, ST6_O + 6:ST6_O + 12], xb[:, 512:1024]).then_inc(cV.h, 1)
            v.wait_ge(cV.h, cV.inc())
            v.bn_aggr(af[:, MV_O + blk * 2:MV_O + blk * 2 + 2],
                      af[:, ST6_O:ST6_O + 12]).then_inc(cV.h, 1)
        aggr = (cV, cV.inc())
        B.aggr_prev = aggr

        @block.scalar
        def _(s, blk=blk):
            _w(s, aggr)
            _w(s, B.sqt_free)
            s.activation(af[:, SQT_O:SQT_O + 1], af[:, MV_O + blk * 2 + 1:MV_O + blk * 2 + 2],
                         AF.Sqrt, bias=af[:, BCL_O + 10:BCL_O + 11], scale=1.0).then_inc(cS.h, 1)
        sq = (cS, cS.inc())

        @block.vector
        def _(v, blk=blk, slot=slot):
            _w(v, sq)
            v.reciprocal(af[:, RSD_O + blk:RSD_O + blk + 1],
                         af[:, SQT_O:SQT_O + 1]).then_inc(cV.h, 1)
            v.wait_ge(cV.h, cV.inc())
            tslot = blk % 2
            _w(v, B.xnt_free[tslot])
            xb = af[:, XB_O + slot * 1024:XB_O + (slot + 1) * 1024]
            v.tensor_scalar(abf[:, XNT_O + tslot * 1024:XNT_O + (tslot + 1) * 1024],
                            xb, af[:, MV_O + blk * 2:MV_O + blk * 2 + 1],
                            af[:, RSD_O + blk:RSD_O + blk + 1],
                            op0=ALU.subtract, op1=ALU.mult).then_inc(cV.h, 1)
        norm = (cV, cV.inc())
        B.xblk_free[slot] = norm
        B.sqt_free = norm

        @block.tensor
        def _(t, blk=blk, g=g):
            tslot = blk % 2
            _w(t, norm)
            _w(t, B.b7_free)
            if blk == 0:
                _w(t, B.consts)
            for dc in range(DC):
                ins = t.transpose(pbt[:, dc * 128:(dc + 1) * 128],
                                  abf[:, XNT_O + tslot * 1024 + dc * 128:
                                      XNT_O + tslot * 1024 + (dc + 1) * 128],
                                  abf[:, IDT_O:IDT_O + 128])
            ins.then_inc(cT.h, 1)
        tp = (cT, cT.inc())

        @block.vector
        def _(v, blk=blk):
            _w(v, tp)
            if blk == 0:
                _w(v, B.xn_free)
            v.tensor_copy(abf[:, XN_O:XN_O + 16384]
                          .rearrange("p (c n) -> p c n", c=DC)[:, :, blk * 128:(blk + 1) * 128],
                          pbt[:, 0:1024].rearrange("p (c n) -> p c n", c=DC)).then_inc(cV.h, 1)
        ev = (cV, cV.inc())
        B.b7_free = ev
        B.xnt_free[blk % 2] = ev
    B.xn_free = []
    return (cV, cV.n)


def ffn_phase(B, dr, src, dst):
    abf = B.abf
    block = B.block
    dWI, dWO = B.dWI, B.dWO

    def dma_wi(e):
        @block.gpsimd
        def _(gp):
            _w(gp, B.wi_free)
            for half in range(2):
                gp.dma_start(
                    out=abf[:, WI_O + half * 8192:WI_O + (half + 1) * 8192]
                    .rearrange("p (c h) -> p c h", c=4),
                    in_=dr['w_in'].ap()[e].rearrange("(c p) h -> p c h", p=128)
                    [:, half * 4:(half + 1) * 4, :]).then_inc(dWI.h, 16)
        B.wi_ready = (dWI, dWI.inc(32))

    def dma_wo(e):
        @block.gpsimd
        def _(gp):
            _w(gp, B.wo_free)
            for half in range(2):
                gp.dma_start(
                    out=abf[:, WO_O + half * 8192:WO_O + (half + 1) * 8192]
                    .rearrange("p (c h) -> p c h", c=8),
                    in_=dr['w_out'].ap()[e].rearrange("(c p) h -> p c h", p=128)
                    [:, half * 8:(half + 1) * 8, :]).then_inc(dWO.h, 16)
        B.wo_ready = (dWO, dWO.inc(32))

    dma_wi(0)
    dma_wo(0)

    for g in range(NGRP):
        xn_ready = ln_pass(B, dr, src, g)
        gates_done = _emit_gating(B, g, xn_ready)
        for e in range(NE):
            _emit_expert(B, g, e, gates_done, xn_ready, dma_wi, dma_wo)
        _emit_residual(B, src, dst, g)


def _emit_gating(B, g, xn_ready):
    abf, af, pb = B.abf, B.af, B.pb
    block = B.block
    cT, cV, cS = B.cT, B.cV, B.cS

    gfm_marks = []
    for mc in range(2):
        for ti in range(NTIL):
            bank = 2 + ((mc * NTIL + ti) % 2)

            @block.tensor
            def _(t, mc=mc, ti=ti, bank=bank):
                if mc == 0 and ti == 0:
                    _w(t, xn_ready)
                    _w(t, B.consts)
                _w(t, B.bank_free[bank])
                for dc in range(DC):
                    ins = t.matmul(pb[bank][:, 0:512],
                                   abf[:, KSB_O + dc * 256 + mc * 128:KSB_O + dc * 256 + (mc + 1) * 128],
                                   abf[:, XN_O + dc * 2048 + ti * 512:XN_O + dc * 2048 + (ti + 1) * 512],
                                   start=(dc == 0), stop=(dc == DC - 1))
                ins.then_inc(cT.h, 1)
            mm = (cT, cT.inc())

            @block.scalar
            def _(s, mc=mc, ti=ti, bank=bank):
                _w(s, mm)
                s.activation(abf[:, HT_O + mc * 2048 + ti * 512:HT_O + mc * 2048 + (ti + 1) * 512],
                             pb[bank][:, 0:512], AF.Gelu,
                             bias=af[:, BCL_O + 8 + mc:BCL_O + 9 + mc], scale=1.0).then_inc(cS.h, 1)
            ev = (cS, cS.inc())
            B.bank_free[bank] = ev
            gfm_marks.append(ev)

    @block.tensor
    def _(t):
        _w(t, gfm_marks)
        _w(t, B.bank_free[4])
        for blk in range(NBLK):
            for gc in range(2):
                ins = t.matmul(pb[4][:, blk * 8:(blk + 1) * 8],
                               abf[:, HT_O + gc * 2048 + blk * 128:HT_O + gc * 2048 + (blk + 1) * 128],
                               abf[:, KSB_O + 2048 + gc * 8:KSB_O + 2048 + (gc + 1) * 8],
                               start=(gc == 0), stop=(gc == 1))
        ins.then_inc(cT.h, 1)
    lg_mm = (cT, cT.inc())

    def vstep(v, ins):
        ins.then_inc(cV.h, 1)
        v.wait_ge(cV.h, cV.inc())

    @block.vector
    def _(v):
        _w(v, lg_mm)
        logi = af[:, LOGI_O:LOGI_O + 128]
        logi3 = logi.rearrange("p (b e) -> p b e", e=8)
        mb = af[:, MB_O:MB_O + 128]
        mb3 = mb.rearrange("p (b e) -> p b e", e=8)
        eq = af[:, EQ_O:EQ_O + 128]
        ex = af[:, EX_O:EX_O + 128]
        vstep(v, v.tensor_copy(logi, pb[4][:, 0:128]))
        vstep(v, v.tensor_reduce(af[:, M1_O:M1_O + 16], logi3, axis=AX.X, op=ALU.max))
        vstep(v, v.tensor_copy(mb3, af[:, M1_O:M1_O + 16].unsqueeze(2).broadcast_to([128, 16, 8])))
        vstep(v, v.tensor_tensor(eq, logi, mb, op=ALU.is_equal))
        vstep(v, v.scalar_tensor_tensor(ex, eq, -1e30, logi, op0=ALU.mult, op1=ALU.add))
        vstep(v, v.tensor_reduce(af[:, M2_O:M2_O + 16],
                                 ex.rearrange("p (b e) -> p b e", e=8), axis=AX.X, op=ALU.max))
        vstep(v, v.tensor_copy(mb3, af[:, M2_O:M2_O + 16].unsqueeze(2).broadcast_to([128, 16, 8])))
        vstep(v, v.tensor_tensor(eq, logi, mb, op=ALU.is_ge))
    pre_exp = cV.mark()

    @block.scalar
    def _(s):
        _w(s, pre_exp)
        s.activation(af[:, EX_O:EX_O + 128], af[:, LOGI_O:LOGI_O + 128], AF.Exp).then_inc(cS.h, 1)
    exp_done = (cS, cS.inc())

    @block.vector
    def _(v):
        _w(v, exp_done)
        ex = af[:, EX_O:EX_O + 128]
        mb = af[:, MB_O:MB_O + 128]
        vstep(v, v.tensor_reduce(af[:, SS_O:SS_O + 16],
                                 ex.rearrange("p (b e) -> p b e", e=8), axis=AX.X, op=ALU.add))
        vstep(v, v.tensor_scalar(af[:, SS_O:SS_O + 16], af[:, SS_O:SS_O + 16],
                                 2.0, None, op0=ALU.mult))
        vstep(v, v.reciprocal(af[:, RH_O:RH_O + 16], af[:, SS_O:SS_O + 16]))
        vstep(v, v.tensor_copy(mb.rearrange("p (b e) -> p b e", e=8),
                               af[:, RH_O:RH_O + 16].unsqueeze(2).broadcast_to([128, 16, 8])))
        vstep(v, v.tensor_tensor(af[:, LOGI_O:LOGI_O + 128], ex, mb, op=ALU.mult))
        v.tensor_tensor(af[:, GATES_O:GATES_O + 128], af[:, LOGI_O:LOGI_O + 128],
                        af[:, EQ_O:EQ_O + 128], op=ALU.mult).then_inc(cV.h, 1)
    gates_done = (cV, cV.inc())
    B.bank_free[4] = gates_done
    return gates_done


def _emit_expert(B, g, e, gates_done, xn_ready, dma_wi, dma_wo):
    abf, af, ah, pb = B.abf, B.af, B.ah, B.pb
    block = B.block
    cT, cV, cS = B.cT, B.cV, B.cS

    HB = [0, 1, 2]
    OP = [(3, 4), (5, 6)]
    wi_ready = B.wi_ready
    wo_ready = B.wo_ready
    last_h_mm = None
    last_o_mm = None

    for ti in range(NTIL):
        hevac = []
        for hc in range(HC):
            bank = HB[(ti * HC + hc) % 3]

            @block.tensor
            def _(t, ti=ti, hc=hc, bank=bank):
                if ti == 0 and hc == 0:
                    _w(t, wi_ready)
                    _w(t, xn_ready)
                _w(t, B.bank_free[bank])
                for dc in range(DC):
                    ins = t.matmul(pb[bank][:, 0:512],
                                   abf[:, WI_O + dc * 2048 + hc * 128:WI_O + dc * 2048 + (hc + 1) * 128],
                                   abf[:, XN_O + dc * 2048 + ti * 512:XN_O + dc * 2048 + (ti + 1) * 512],
                                   start=(dc == 0), stop=(dc == DC - 1))
                ins.then_inc(cT.h, 1)
            mm = (cT, cT.inc())
            if ti == NTIL - 1 and hc == HC - 1:
                last_h_mm = mm

            @block.scalar
            def _(s, hc=hc, bank=bank):
                _w(s, mm)
                s.activation(abf[:, HT_O + hc * 512:HT_O + (hc + 1) * 512],
                             pb[bank][:, 0:512], AF.Gelu).then_inc(cS.h, 1)
            ev = (cS, cS.inc())
            B.bank_free[bank] = ev
            hevac.append(ev)

        for sb in range(4):
            blkb = ti * 4 + sb
            for half in range(2):
                bank = OP[sb % 2][half]

                @block.tensor
                def _(t, sb=sb, half=half, bank=bank):
                    if sb == 0 and half == 0:
                        _w(t, hevac)
                        if ti == 0:
                            _w(t, wo_ready)
                            _w(t, gates_done)
                    _w(t, B.bank_free[bank])
                    for kc in range(HC):
                        ins = t.matmul(pb[bank][:, 0:512],
                                       abf[:, HT_O + kc * 512 + sb * 128:HT_O + kc * 512 + (sb + 1) * 128],
                                       abf[:, WO_O + kc * 1024 + half * 512:WO_O + kc * 1024 + (half + 1) * 512],
                                       start=(kc == 0), stop=(kc == HC - 1))
                    ins.then_inc(cT.h, 1)
                omm = (cT, cT.inc())
                if ti == NTIL - 1 and sb == 3 and half == 1:
                    last_o_mm = omm

                @block.vector
                def _(v, blkb=blkb, half=half, bank=bank, e=e):
                    _w(v, omm)
                    ya = ah[:, blkb * 1024 + half * 512:blkb * 1024 + (half + 1) * 512]
                    gate = af[:, GATES_O + blkb * 8 + e:GATES_O + blkb * 8 + e + 1]
                    if e == 0:
                        v.tensor_scalar(ya, pb[bank][:, 0:512], gate, None,
                                        op0=ALU.mult).then_inc(cV.h, 1)
                    else:
                        v.scalar_tensor_tensor(ya, pb[bank][:, 0:512], gate, ya,
                                               op0=ALU.mult, op1=ALU.add).then_inc(cV.h, 1)
                oev = (cV, cV.inc())
                B.bank_free[bank] = oev

    B.wi_free = last_h_mm
    B.wo_free = last_o_mm
    if e == NE - 1:
        if g + 1 < NGRP:
            dma_wi(0)
            dma_wo(0)
    else:
        dma_wi(e + 1)
        dma_wo(e + 1)
    B.ya_done = cV.mark()
    B.xn_free = [last_h_mm]


def _emit_residual(B, src, dst, g):
    af, ah = B.af, B.ah
    block = B.block
    cV = B.cV
    ya_done = B.ya_done

    for blk in range(NBLK):
        slot = blk % 3
        dxb = B.dXB[slot]
        tok0 = g * GTOK + blk * 128

        @block.sync
        def _(sy, slot=slot, tok0=tok0):
            _w(sy, B.xblk_free[slot])
            sy.dma_start(out=af[:, XB_O + slot * 1024:XB_O + (slot + 1) * 1024],
                         in_=src.ap()[tok0:tok0 + 128, :]).then_inc(dxb.h, 16)
        ld = (dxb, dxb.inc(16))

        lslot = blk % 2
        lns = LNS_O if lslot == 0 else LNS2_O

        @block.vector
        def _(v, blk=blk, slot=slot, lns=lns, lslot=lslot):
            _w(v, ld)
            if blk == 0:
                _w(v, ya_done)
            _w(v, B.lns_free[lslot])
            v.tensor_tensor(af[:, lns:lns + 1024],
                            af[:, XB_O + slot * 1024:XB_O + (slot + 1) * 1024],
                            ah[:, blk * 1024:(blk + 1) * 1024], op=ALU.add).then_inc(cV.h, 1)
        add = (cV, cV.inc())
        B.xblk_free[slot] = add

        dst_sem = B.dST[lslot]

        @block.sync
        def _(sy, tok0=tok0, lns=lns):
            _w(sy, add)
            sy.dma_start(out=dst.ap()[tok0:tok0 + 128, :],
                         in_=af[:, lns:lns + 1024]).then_inc(dst_sem.h, 16)
        B.lns_free[lslot] = (dst_sem, dst_sem.inc(16))


def attn_phase(B, dr, src, dst, b1_only=False):
    abf, af, pb = B.abf, B.af, B.pb
    block = B.block
    cT, cV, cS, cCC = B.cT, B.cV, B.cS, B.cCC
    dWI, dWO, dXN, dCC = B.dWI, B.dWO, B.dXN, B.dCC

    @block.gpsimd
    def _(gp):
        _w(gp, B.wo_free)
        for name, off in [('wk', WO_O), ('wv', WO_O + 8192)]:
            gp.dma_start(out=abf[:, off:off + 8192].rearrange("p (c h) -> p c h", c=DC),
                         in_=dr[name].ap().rearrange("(c p) h -> p c h", p=128)).then_inc(dWO.h, 16)
    wkv_ready = (dWO, dWO.inc(32))

    @block.gpsimd
    def _(gp):
        _w(gp, B.wi_free)
        for name, off in [('wq', WI_O), ('wh', WI_O + 8192)]:
            gp.dma_start(out=abf[:, off:off + 8192].rearrange("p (c h) -> p c h", c=DC),
                         in_=dr[name].ap().rearrange("(c p) h -> p c h", p=128)).then_inc(dWI.h, 16)
    wqh_ready = (dWI, dWI.inc(32))

    kv_marks = []
    kv_last_mm = None
    for g in range(NGRP):
        xn_ready = ln_pass(B, dr, src, g)

        @block.sync
        def _(sy, g=g):
            _w(sy, xn_ready)
            sy.dma_start(out=dr['xnscr'].ap()[g], in_=abf[:, XN_O:XN_O + 16384]).then_inc(dXN.h, 16)
        xn_saved = (dXN, dXN.inc(16))

        @block.vector
        def _(v, g=g):
            v.memset(af[:, KVS_O + g * 520:KVS_O + (g + 1) * 520], 0.0).then_inc(cV.h, 1)
        kv_acc = (cV, cV.inc())

        for blk in range(NBLK):
            @block.tensor
            def _(t, blk=blk):
                if blk == 0:
                    _w(t, wkv_ready)
                    _w(t, xn_ready)
                for half in range(2):
                    _w(t, B.bank_free[half])
                    for dc in range(DC):
                        t.matmul(pb[half][:, 0:512],
                                 abf[:, XN_O + dc * 2048 + blk * 128:XN_O + dc * 2048 + (blk + 1) * 128],
                                 abf[:, WO_O + dc * 1024 + half * 512:WO_O + dc * 1024 + (half + 1) * 512],
                                 start=(dc == 0), stop=False)
                    ins = t.matmul(pb[half][:, 0:512],
                                   abf[0:1, ONE_O:ONE_O + 128],
                                   abf[0:1, BROW_O + half * 512:BROW_O + (half + 1) * 512],
                                   start=False, stop=True)
                ins.then_inc(cT.h, 1)
            kmm = (cT, cT.inc())

            @block.scalar
            def _(s):
                _w(s, kmm)
                for half in range(2):
                    ins = s.activation(abf[:, K_SB_O + half * 512:K_SB_O + (half + 1) * 512],
                                       pb[half][:, 0:512], AF.Sigmoid)
                ins.then_inc(cS.h, 1)
            kev = (cS, cS.inc())
            B.bank_free[0] = kev
            B.bank_free[1] = kev

            @block.tensor
            def _(t, blk=blk):
                for half in range(2):
                    _w(t, B.bank_free[half])
                    for dc in range(DC):
                        t.matmul(pb[half][:, 0:512],
                                 abf[:, XN_O + dc * 2048 + blk * 128:XN_O + dc * 2048 + (blk + 1) * 128],
                                 abf[:, WO_O + 8192 + dc * 1024 + half * 512:
                                     WO_O + 8192 + dc * 1024 + (half + 1) * 512],
                                 start=(dc == 0), stop=False)
                    ins = t.matmul(pb[half][:, 0:512],
                                   abf[32:33, ONE_O:ONE_O + 128],
                                   abf[32:33, BROW_O + half * 512:BROW_O + (half + 1) * 512],
                                   start=False, stop=True)
                ins.then_inc(cT.h, 1)
            vmm = (cT, cT.inc())

            @block.scalar
            def _(s):
                _w(s, vmm)
                for half in range(2):
                    ins = s.copy(
                        abf[:, V_SB_O:V_SB_O + 1040].rearrange("p (h c) -> p h c", c=65)
                        [:, half * 8:(half + 1) * 8, 0:64],
                        pb[half][:, 0:512].rearrange("p (h c) -> p h c", c=64))
                ins.then_inc(cS.h, 1)
            vev = (cS, cS.inc())
            B.bank_free[0] = vev
            B.bank_free[1] = vev

            @block.tensor
            def _(t, blk=blk):
                _w(t, vev)
                _w(t, kv_acc)
                for hp in range(8):
                    bank = 2 + hp // 2
                    col = (hp % 2) * 65
                    for par in range(2):
                        h = 2 * hp + par
                        ins = t.matmul(pb[bank][par * 64:par * 64 + 64, col:col + 65],
                                       abf[:, K_SB_O + h * 64:K_SB_O + (h + 1) * 64],
                                       abf[:, V_SB_O + h * 65:V_SB_O + (h + 1) * 65],
                                       start=True, stop=True,
                                       tile_position=(0, 64 * par))
                ins.then_inc(cT.h, 1)
            kvmm = (cT, cT.inc())
            if g == NGRP - 1 and blk == NBLK - 1:
                kv_last_mm = kvmm

            @block.vector
            def _(v, g=g):
                _w(v, kvmm)
                for j in range(4):
                    ins = v.tensor_tensor(
                        af[:, KVS_O + g * 520 + j * 130:KVS_O + g * 520 + (j + 1) * 130],
                        af[:, KVS_O + g * 520 + j * 130:KVS_O + g * 520 + (j + 1) * 130],
                        pb[2 + j][:, 0:130], op=ALU.add)
                ins.then_inc(cV.h, 1)
            kv_acc = (cV, cV.inc())
            for j in range(4):
                B.bank_free[2 + j] = kv_acc

        kv_marks.append(kv_acc)
        B.xn_free = [kv_last_mm if kv_last_mm else kvmm, xn_saved]

    @block.sync
    def _(sy):
        _w(sy, kv_marks)
        sy.dma_start(out=dr['cc_in'].ap(), in_=af[:, KVS_O:KVS_O + 1040]).then_inc(dCC.h, 16)
    cc_st = (dCC, dCC.inc(16))

    @block.gpsimd
    def _(gp):
        _w(gp, cc_st)
        gp.collective_compute(
            "AllReduce", ALU.add,
            replica_groups=[list(range(NCORE))],
            ins=[dr['cc_in'].ap().opt()],
            outs=[dr['cc_out'].ap().opt()],
        ).then_inc(cCC.h, 1)
    cc_done = (cCC, cCC.inc())

    @block.sync
    def _(sy):
        _w(sy, cc_done)
        sy.dma_start(out=af[:, KVS_O:KVS_O + 1040], in_=dr['cc_out'].ap()).then_inc(dCC.h, 16)
    cc_ld = (dCC, dCC.inc(16))

    @block.vector
    def _(v):
        _w(v, cc_ld)
        v.tensor_copy(abf[:, KVBF_O:KVBF_O + 1040], af[:, KVS_O:KVS_O + 1040]).then_inc(cV.h, 1)
    kvbf = (cV, cV.inc())

    if b1_only:
        @block.sync
        def _(sy):
            _w(sy, kvbf)
            sy.dma_start(out=dst.ap()[0:128, :], in_=af[:, KVS_O:KVS_O + 1024]).then_inc(B.dST[0].h, 16)
            sy.dma_start(out=dst.ap()[128:256, 0:16], in_=af[:, KVS_O + 1024:KVS_O + 1040]).then_inc(B.dST[0].h, 16)
        B.dST[0].inc(32)
        return

    @block.gpsimd
    def _(gp):
        _w(gp, [kv_last_mm])
        gp.dma_start(out=abf[:, WO_O:WO_O + 8192].rearrange("p (c h) -> p c h", c=DC),
                     in_=dr['wo_a'].ap().rearrange("(c p) h -> p c h", p=128)).then_inc(dWO.h, 16)
    woa_ready = (dWO, dWO.inc(16))

    for g in range(NGRP):
        _emit_b2(B, dr, src, dst, g, kvbf, wqh_ready, woa_ready)


def _emit_b2(B, dr, src, dst, g, kvbf, wqh_ready, woa_ready):
    abf, af, ah, pb, pbt = B.abf, B.af, B.ah, B.pb, B.pbt
    block = B.block
    cT, cV, cS, cG = B.cT, B.cV, B.cS, B.cG
    dXN = B.dXN

    def qfm(oc, c0, c1):
        if oc < 4:
            base = WO_O + 8192 + oc * 2048
        else:
            base = HT_O + (oc - 4) * 2048
        return abf[:, base + c0:base + c1]

    @block.sync
    def _(sy, g=g):
        _w(sy, B.xn_free)
        sy.dma_start(out=abf[:, XN_O:XN_O + 16384], in_=dr['xnscr'].ap()[g]).then_inc(dXN.h, 16)
    xn_ld = (dXN, dXN.inc(16))

    q_marks = []
    last_q_mm = None
    for oc in range(DC):
        for ti in range(NTIL):
            bank = 5 + ((oc * NTIL + ti) % 2)

            @block.tensor
            def _(t, oc=oc, ti=ti, bank=bank):
                if oc == 0 and ti == 0:
                    _w(t, xn_ld)
                    _w(t, wqh_ready)
                _w(t, B.bank_free[bank])
                for kc in range(DC):
                    ins = t.matmul(pb[bank][:, 0:512],
                                   abf[:, WI_O + kc * 1024 + oc * 128:WI_O + kc * 1024 + (oc + 1) * 128],
                                   abf[:, XN_O + kc * 2048 + ti * 512:XN_O + kc * 2048 + (ti + 1) * 512],
                                   start=(kc == 0), stop=(kc == DC - 1))
                ins.then_inc(cT.h, 1)
            mm = (cT, cT.inc())
            last_q_mm = mm

            @block.scalar
            def _(s, oc=oc, ti=ti, bank=bank):
                _w(s, mm)
                s.activation(qfm(oc, ti * 512, (ti + 1) * 512), pb[bank][:, 0:512],
                             AF.Sigmoid, bias=af[:, BCL_O + oc:BCL_O + oc + 1],
                             scale=1.0).then_inc(cS.h, 1)
            ev = (cS, cS.inc())
            B.bank_free[bank] = ev
            q_marks.append(ev)

    if B2_DEPTH <= 1:
        @block.gpsimd
        def _(gp):
            _w(gp, q_marks)
            gp.dma_start(out=dst.ap()[g * 2048:g * 2048 + 128, :], in_=qfm(0, 0, 1024)).then_inc(B.dC.h, 16)
        B.dC.inc(16)
        B.xn_free = [last_q_mm]
        B.wi_free = last_q_mm
        B.wo_free = last_q_mm
        return

    last_h_mm = None
    last_o_mm = None
    for blk in range(NBLK):
        tok0 = g * GTOK + blk * 128
        slot = blk % 3
        dxb = B.dXB[slot]

        @block.sync
        def _(sy, slot=slot, tok0=tok0):
            _w(sy, B.xblk_free[slot])
            sy.dma_start(out=af[:, XB_O + slot * 1024:XB_O + (slot + 1) * 1024],
                         in_=src.ap()[tok0:tok0 + 128, :]).then_inc(dxb.h, 16)
        xld = (dxb, dxb.inc(16))

        @block.tensor
        def _(t, blk=blk, g=g):
            if blk == 0:
                _w(t, kvbf)
                _w(t, q_marks)
            for j in range(4):
                _w(t, B.bank_free[j])
            ti, sb = blk // 4, blk % 4
            for hp in range(8):
                for par in range(2):
                    bank = par * 2 + hp // 4
                    s = hp % 4
                    lhs = qfm(hp, ti * 512 + sb * 128, ti * 512 + (sb + 1) * 128)
                    ins = t.matmul(
                        pb[bank][:, s * 65:s * 65 + 65],
                        lhs[par * 64:par * 64 + 64, :],
                        abf[par * 64:par * 64 + 64,
                            KVBF_O + g * 520 + hp * 65:KVBF_O + g * 520 + (hp + 1) * 65],
                        start=True, stop=True, tile_position=(64 * par, 0))
            ins.then_inc(cT.h, 1)
        nmm = (cT, cT.inc())

        if B2_DEPTH == 15:
            @block.vector
            def _(v, blk=blk):
                _w(v, nmm)
                for j in range(4):
                    ins = v.tensor_copy(ah[:, blk * 1000 + j * 250:blk * 1000 + j * 250 + 250],
                                        pb[j][:, 0:250])
                ins.then_inc(cV.h, 1)
            attnev = (cV, cV.inc())
            for j in range(4):
                B.bank_free[j] = attnev
            B.aggr_prev = attnev
            last_h_mm = nmm
            last_o_mm = nmm
            continue

        @block.vector
        def _(v, blk=blk):
            _w(v, nmm)
            for j in range(4):
                ins = v.tensor_copy(af[:, DEN_O + j * 4:DEN_O + (j + 1) * 4],
                                    pb[j][:, 0:260].rearrange("p (s c) -> p s c", c=65)[:, :, 64])
            ins.then_inc(cV.h, 1)
            v.wait_ge(cV.h, cV.inc())
            v.reciprocal(af[:, RDN_O:RDN_O + 16], af[:, DEN_O:DEN_O + 16]).then_inc(cV.h, 1)
            v.wait_ge(cV.h, cV.inc())
            for h in range(16):
                par, hp = h % 2, h // 2
                bank = par * 2 + hp // 4
                s = hp % 4
                ins = v.tensor_scalar(ah[:, blk * 1024 + h * 64:blk * 1024 + (h + 1) * 64],
                                      pb[bank][:, s * 65:s * 65 + 64],
                                      af[:, RDN_O + bank * 4 + s:RDN_O + bank * 4 + s + 1],
                                      None, op0=ALU.mult)
            ins.then_inc(cV.h, 1)
        attnev = (cV, cV.inc())
        for j in range(4):
            B.bank_free[j] = attnev

        if B2_DEPTH <= 2:
            B.aggr_prev = attnev
            last_h_mm = nmm
            last_o_mm = nmm
            continue

        @block.vector
        def _(v, blk=blk):
            _w(v, attnev)
            _w(v, B.aggr_prev)
            ab = ah[:, blk * 1024:(blk + 1) * 1024]
            v.bn_stats(af[:, ST6_O:ST6_O + 6], ab[:, 0:512]).then_inc(cV.h, 1)
            v.wait_ge(cV.h, cV.inc())
            v.bn_stats(af[:, ST6_O + 6:ST6_O + 12], ab[:, 512:1024]).then_inc(cV.h, 1)
            v.wait_ge(cV.h, cV.inc())
            v.bn_aggr(af[:, MV_O + (blk % 2) * 2:MV_O + (blk % 2) * 2 + 2],
                      af[:, ST6_O:ST6_O + 12]).then_inc(cV.h, 1)
        aggr = (cV, cV.inc())
        B.aggr_prev = aggr

        @block.scalar
        def _(s, blk=blk):
            _w(s, aggr)
            _w(s, B.sqt_free)
            s.activation(af[:, SQT_O:SQT_O + 1], af[:, MV_O + (blk % 2) * 2 + 1:MV_O + (blk % 2) * 2 + 2],
                         AF.Sqrt, bias=af[:, BCL_O + 10:BCL_O + 11], scale=1.0).then_inc(cS.h, 1)
        sq = (cS, cS.inc())

        lslot = blk % 2
        lns = LNS_O if lslot == 0 else LNS2_O
        lns2 = LNS2_O if lslot == 0 else LNS_O

        @block.vector
        def _(v, blk=blk, lns=lns, lslot=lslot):
            _w(v, sq)
            v.reciprocal(af[:, RSD_O:RSD_O + 1], af[:, SQT_O:SQT_O + 1]).then_inc(cV.h, 1)
            v.wait_ge(cV.h, cV.inc())
            _w(v, B.lns_free[lslot])
            v.tensor_scalar(af[:, lns:lns + 1024], ah[:, blk * 1024:(blk + 1) * 1024],
                            af[:, MV_O + (blk % 2) * 2:MV_O + (blk % 2) * 2 + 1],
                            af[:, RSD_O:RSD_O + 1],
                            op0=ALU.subtract, op1=ALU.mult).then_inc(cV.h, 1)
        t1 = (cV, cV.inc())
        B.sqt_free = t1

        @block.vector
        def _(v, lns=lns, lns2=lns2, lslot=lslot):
            _w(v, t1)
            _w(v, B.lns_free[1 - lslot])
            v.tensor_tensor(af[:, lns2:lns2 + 1024], af[:, lns:lns + 1024],
                            abf[:, BCB_O:BCB_O + 1024], op=ALU.mult).then_inc(cV.h, 1)
            v.wait_ge(cV.h, cV.inc())
            v.tensor_tensor(af[:, lns:lns + 1024], af[:, lns2:lns2 + 1024],
                            abf[:, BCB_O + 1024:BCB_O + 2048], op=ALU.add).then_inc(cV.h, 1)
        ln_done = (cV, cV.inc())

        @block.tensor
        def _(t, blk=blk):
            for half in range(2):
                bank = 5 + half
                _w(t, B.bank_free[bank])
                t.matmul(pb[bank][:, 0:512],
                         abf[32:33, ONE_O:ONE_O + 128],
                         abf[32:33, BROW_O + 1024 + half * 512:BROW_O + 1024 + (half + 1) * 512],
                         start=True, stop=False)
                for dc in range(DC):
                    t.matmul(pb[bank][:, 0:512],
                             abf[:, XN_O + dc * 2048 + blk * 128:XN_O + dc * 2048 + (blk + 1) * 128],
                             abf[:, WI_O + 8192 + dc * 1024 + half * 512:
                                 WI_O + 8192 + dc * 1024 + (half + 1) * 512],
                             start=False, stop=False)
                ins = t.matmul(pb[bank][:, 0:512],
                               abf[0:1, ONE_O:ONE_O + 128],
                               abf[0:1, BROW_O + 1024 + half * 512:BROW_O + 1024 + (half + 1) * 512],
                               start=False, stop=True)
            ins.then_inc(cT.h, 1)
        hmm = (cT, cT.inc())
        last_h_mm = hmm

        @block.scalar
        def _(s, blk=blk):
            _w(s, hmm)
            _w(s, B.xnt_free[blk % 2])
            for half in range(2):
                ins = s.copy(abf[:, XNT_O + (blk % 2) * 1024 + half * 512:
                                 XNT_O + (blk % 2) * 1024 + (half + 1) * 512],
                             pb[5 + half][:, 0:512])
            ins.then_inc(cS.h, 1)
        hev = (cS, cS.inc())
        B.bank_free[5] = hev
        B.bank_free[6] = hev

        @block.vector
        def _(v, blk=blk, lns=lns):
            _w(v, ln_done)
            _w(v, hev)
            _w(v, B.gtm_free[blk % 2])
            v.tensor_tensor(abf[:, GTM_O + (blk % 2) * 1024:GTM_O + (blk % 2 + 1) * 1024],
                            af[:, lns:lns + 1024],
                            abf[:, XNT_O + (blk % 2) * 1024:XNT_O + (blk % 2 + 1) * 1024],
                            op=ALU.mult).then_inc(cV.h, 1)
        gated = (cV, cV.inc())
        B.xnt_free[blk % 2] = gated
        lns_consumed = gated

        if B2_DEPTH <= 3:
            B.lns_free[lslot] = gated
            B.lns_free[1 - lslot] = gated
            last_o_mm = hmm
            continue

        @block.tensor
        def _(t, blk=blk):
            _w(t, gated)
            _w(t, B.b7_free)
            for dc in range(DC):
                ins = t.transpose(pbt[:, dc * 128:(dc + 1) * 128],
                                  abf[:, GTM_O + (blk % 2) * 1024 + dc * 128:
                                      GTM_O + (blk % 2) * 1024 + (dc + 1) * 128],
                                  abf[:, IDT_O:IDT_O + 128])
            ins.then_inc(cT.h, 1)
        tp = (cT, cT.inc())
        B.gtm_free[blk % 2] = tp

        @block.vector
        def _(v, blk=blk):
            _w(v, tp)
            _w(v, B.gfm_free[blk % 2])
            v.tensor_copy(abf[:, GFM_O + (blk % 2) * 1024:GFM_O + (blk % 2 + 1) * 1024],
                          pbt[:, 0:1024]).then_inc(cV.h, 1)
        gfm = (cV, cV.inc())
        B.b7_free = gfm

        @block.tensor
        def _(t, blk=blk):
            if blk == 0:
                _w(t, woa_ready)
            _w(t, gfm)
            for half in range(2):
                bank = 5 + half
                _w(t, B.bank_free[bank])
                for dc in range(DC):
                    t.matmul(pb[bank][:, 0:512],
                             abf[:, GFM_O + (blk % 2) * 1024 + dc * 128:
                                 GFM_O + (blk % 2) * 1024 + (dc + 1) * 128],
                             abf[:, WO_O + dc * 1024 + half * 512:WO_O + dc * 1024 + (half + 1) * 512],
                             start=(dc == 0), stop=False)
                ins = t.matmul(pb[bank][:, 0:512],
                               abf[64:65, ONE_O:ONE_O + 128],
                               abf[64:65, BROW_O + half * 512:BROW_O + (half + 1) * 512],
                               start=False, stop=True)
            ins.then_inc(cT.h, 1)
        omm = (cT, cT.inc())
        last_o_mm = omm
        B.gfm_free[blk % 2] = omm

        @block.scalar
        def _(s, blk=blk, lns2=lns2, lslot=lslot):
            _w(s, omm)
            _w(s, B.lns_free[1 - lslot])
            for half in range(2):
                ins = s.activation(af[:, lns2 + half * 512:lns2 + (half + 1) * 512],
                                   pb[5 + half][:, 0:512], AF.Relu)
            ins.then_inc(cS.h, 1)
        relu = (cS, cS.inc())
        B.bank_free[5] = relu
        B.bank_free[6] = relu

        @block.vector
        def _(v, blk=blk, slot=slot, lns=lns, lns2=lns2):
            _w(v, relu)
            _w(v, xld)
            _w(v, lns_consumed)
            v.tensor_tensor(af[:, lns:lns + 1024],
                            af[:, XB_O + slot * 1024:XB_O + (slot + 1) * 1024],
                            af[:, lns2:lns2 + 1024], op=ALU.add).then_inc(cV.h, 1)
        add = (cV, cV.inc())
        B.xblk_free[slot] = add
        B.lns_free[1 - lslot] = add

        dst_sem = B.dST[lslot]

        @block.sync
        def _(sy, tok0=tok0, lns=lns):
            _w(sy, add)
            sy.dma_start(out=dst.ap()[tok0:tok0 + 128, :],
                         in_=af[:, lns:lns + 1024]).then_inc(dst_sem.h, 16)
        B.lns_free[lslot] = (dst_sem, dst_sem.inc(16))

    if B2_DEPTH in (2, 15):
        @block.gpsimd
        def _(gp):
            _w(gp, B.aggr_prev)
            gp.dma_start(out=dst.ap()[g * 2048:g * 2048 + 128, :], in_=ah[:, 0:1024]).then_inc(B.dC.h, 16)
        B.dC.inc(16)
    if B2_DEPTH == 3:
        @block.gpsimd
        def _(gp):
            _w(gp, B.lns_free[0])
            gp.dma_start(out=dst.ap()[g * 2048:g * 2048 + 128, :], in_=abf[:, GTM_O:GTM_O + 1024]).then_inc(B.dC.h, 16)
        B.dC.inc(16)
    B.xn_free = [last_h_mm, last_q_mm]
    B.wi_free = last_h_mm if last_h_mm else last_q_mm
    B.wo_free = last_o_mm if last_o_mm else last_q_mm


_PERM = None


def _perm():
    global _PERM
    if _PERM is None:
        p = np.zeros(1024, dtype=np.int64)
        for h in range(16):
            for d in range(64):
                p[h * 64 + d] = d * 16 + h
        _PERM = p
    return _PERM


_NC_CACHE = {}


def _get_program(phases):
    key = tuple(phases)
    if key not in _NC_CACHE:
        _NC_CACHE[key] = build_program(phases)
    return _NC_CACHE[key]


def kernel(**inputs):
    ins = {k: np.ascontiguousarray(np.asarray(v)) for k, v in inputs.items()}
    perm = _perm()
    Wq = ins['L0_attn_Wq'][:, perm]
    Wk = ins['L0_attn_Wk'][:, perm]
    Wv = ins['L0_attn_Wv'][:, perm]
    Wh = ins['L0_attn_Wh'][:, perm]
    Wo = ins['L0_attn_Wo'][perm, :]
    bq = ins['L0_attn_bq'][perm]
    bk = ins['L0_attn_bk'][perm]
    bv = ins['L0_attn_bv'][perm]
    bh = ins['L0_attn_bh'][perm]
    beta = ins['L0_attn_beta'][perm]
    lnw = ins['L0_attn_ln_w'][perm]
    lnb = ins['L0_attn_ln_b'][perm]

    brows = np.zeros((128, 2048), np.float32)
    brows[0, 0:1024] = bk
    brows[32, 0:1024] = bv
    brows[64, 0:1024] = ins['L0_attn_bo']
    brows[0, 1024:2048] = bh
    brows[32, 1024:2048] = beta
    lnrows = np.zeros((128, 1024), np.float32)
    lnrows[0] = lnw
    lnrows[32] = lnb
    bcols = np.zeros((128, 16), np.float32)
    bcols[:, 0:8] = bq.reshape(8, 128).T
    bcols[:, 8:10] = ins['L0_ffn_bg1'].reshape(2, 128).T
    bcols[:, 10] = 1e-5
    ident = np.eye(128, dtype=ml_dtypes.bfloat16)

    x = ins['L0_attn_x']
    nc = _get_program(('A', 'B', 'C'))
    in_maps = []
    for c in range(NCORE):
        sl = slice(c * GTOK, (c + 1) * GTOK)
        xcore = np.ascontiguousarray(
            np.concatenate([x[0, sl], x[1, sl]], axis=0))
        in_maps.append(dict(
            xcore=xcore,
            w_in=ins['L0_ffn_w_in'], w_out=ins['L0_ffn_w_out'],
            wg1=ins['L0_ffn_wg1'], wg2=ins['L0_ffn_wg2'],
            wq=np.ascontiguousarray(Wq), wk=np.ascontiguousarray(Wk),
            wv=np.ascontiguousarray(Wv), wh=np.ascontiguousarray(Wh),
            wo_a=np.ascontiguousarray(Wo),
            brows=brows, lnrows=lnrows, bcols=bcols, ident=ident))
    res = run_bass_kernel_spmd(nc, in_maps, core_ids=list(range(NCORE)))
    out = np.empty((2, 16384, 1024), np.float32)
    for c in range(NCORE):
        xo = res.results[c]['xout']
        out[0, c * GTOK:(c + 1) * GTOK] = xo[0:GTOK]
        out[1, c * GTOK:(c + 1) * GTOK] = xo[GTOK:TOK]
    return out
